# version 02 (final)
# baseline (speedup 1.0000x reference)
import contextlib
import os
import numpy as np
import ml_dtypes

import concourse.bass as bass
import concourse.mybir as mybir
from concourse.bass_utils import run_bass_kernel_spmd
from concourse.hw_specs import get_activation_tables

F32 = mybir.dt.float32
BF16 = mybir.dt.bfloat16
AF = mybir.ActivationFunctionType
ALU = mybir.AluOpType
AX = mybir.AxisListType

D = 1024
H = 16
DH = 64
DI = 2730
NP = 22
NL = 6
EPS = 1e-5
ENGS = ("PE", "ACT", "DVE", "POOL", "SP")
RACE_DETECT = False
P1CUT = int(os.environ.get("K_P1CUT", "0"))
HQN = int(os.environ.get("K_HQN", "4"))
NSL = int(os.environ.get("K_NSL", "99"))
P1DMA = os.environ.get("K_P1DMA", "ab")
P0MODE = os.environ.get("K_P0", "full")
PJF = int(os.environ.get("K_PJF", "4"))
PJK = int(os.environ.get("K_PJK", "8"))
PJB = int(os.environ.get("K_PJB", "1"))
PJE = int(os.environ.get("K_PJE", "1"))


class KB:
    def __init__(self, T):
        self.T = T
        self.NT = T // 128
        self.NS = T // 512
        self.ZCH = 512
        self.NCH = T // self.ZCH
        self.ZNT = self.ZCH // 128
        self.ZNS = self.ZCH // 512
        self.nc = bass.Bass("TRN2", target_bir_lowering=False,
                            detect_race_conditions=RACE_DETECT)
        self.ops = {e: [] for e in ENGS}
        self.cnt = {}
        self.sem_handles = {}
        self.es = contextlib.ExitStack()
        self._pend = {e: {} for e in ENGS}
        self._last_comp = {}

    def sem(self, name):
        if name not in self.sem_handles:
            self.sem_handles[name] = self.es.enter_context(self.nc.semaphore(name))
            self.cnt.setdefault(name, 0)
        return self.sem_handles[name]

    def wait(self, eng, tok):
        if tok is None:
            return
        s, v = tok
        d = self._pend[eng]
        if d.get(s, 0) < v:
            d[s] = v

    def op(self, eng, fn, inc=None, _dma=False):
        waits = self._pend[eng]
        self._pend[eng] = {}
        if eng in ("DVE", "POOL"):
            lc = self._last_comp.get(eng)
            if lc is not None and lc[1] is None:
                idx = lc[0]
                pfn, pw, _, _ = self.ops[eng][idx]
                s = "s" + eng
                self.sem(s)
                self.cnt[s] = self.cnt.get(s, 0) + 1
                ptok = (s, self.cnt[s])
                self.ops[eng][idx] = (pfn, pw, s, ptok)
                self._last_comp[eng] = (idx, ptok)
                lc = self._last_comp[eng]
            if lc is not None:
                ps, pv = lc[1]
                if waits.get(ps, 0) < pv:
                    waits[ps] = pv
        tok = None
        if inc is not None:
            amt = 16 if inc.startswith("d") else 1
            self.sem(inc)
            self.cnt[inc] = self.cnt.get(inc, 0) + amt
            tok = (inc, self.cnt[inc])
        self.ops[eng].append((fn, waits, inc, tok))
        if not _dma:
            self._last_comp[eng] = (len(self.ops[eng]) - 1, tok)
        return tok

    def dma(self, eng, out_ap, in_ap, inc):
        self.sem(inc)
        prev = self.cnt.get(inc, 0)
        if prev > 0:
            self.wait(eng, (inc, prev))
        return self.op(eng, lambda e, o=out_ap, i=in_ap: e.dma_start(o, i),
                       inc=inc, _dma=True)

    def act_table(self, idx):
        self._tbl = getattr(self, "_tbl", 0) + 1
        name = "tblld%d" % self._tbl
        self.op("ACT", lambda a, name=name, idx=idx: a.add_instruction(
            mybir.InstLoadActFuncSet(name=name, act_func_set_id=idx,
                                     ins=[], outs=[])))

    def pe_tok(self):
        ops = self.ops["PE"]
        if ops:
            fn, waits, inc, tok = ops[-1]
            if tok is not None:
                return tok
            self.sem("sPE")
            self.cnt["sPE"] = self.cnt.get("sPE", 0) + 1
            tok = ("sPE", self.cnt["sPE"])
            ops[-1] = (fn, waits, "sPE", tok)
            return tok
        return self.op("PE", lambda p: p.nop(), inc="sPE")

    def barrier(self):
        toks = [self.pe_tok()]
        for e in ENGS:
            if e != "PE":
                toks.append(self.op(e, lambda eng: eng.nop(), inc="s" + e))
        for e in ENGS:
            for t in toks:
                self.wait(e, t)

    def finalize_and_emit(self):
        if getattr(self, "act_tail", None) is not None:
            for _ in range(2):
                self.op("ACT", self.act_tail)
        for s, v in list(self.cnt.items()):
            if s.startswith("d"):
                self.wait("SP", (s, v))
        self.op("SP", lambda e: e.nop())
        blk = self.es.enter_context(self.nc.Block())
        sems = self.sem_handles

        def runner(name):
            def run(engine):
                for fn, waits, inc, _tok in self.ops[name]:
                    for s, v in waits.items():
                        engine.wait_ge(sems[s], v)
                    inst = fn(engine)
                    if inc is not None:
                        inst.then_inc(sems[inc], 16 if inc.startswith("d") else 1)
            return run

        blk.tensor(runner("PE"))
        blk.scalar(runner("ACT"))
        blk.vector(runner("DVE"))
        blk.gpsimd(runner("POOL"))
        blk.sync(runner("SP"))


def build_nc(T=4096, nl_run=NL, attn_only_last=False, phases="12345"):
    kb = KB(T)
    _tabs = list(get_activation_tables(kb.nc.m.arch))
    TBL_NLE = _tabs.index("natural_log_exp_and_others")
    TBL_GEL = _tabs.index("gelu_and_others")
    kb.act_table(TBL_NLE)
    nc = kb.nc
    NT, NS = kb.NT, kb.NS
    ZCH, NCH, ZNT, ZNS = kb.ZCH, kb.NCH, kb.ZNT, kb.ZNS

    def dram(n, s, dt, k=None):
        return nc.dram_tensor(n, list(s), dt) if k is None else \
            nc.dram_tensor(n, list(s), dt, kind=k)

    x0 = dram("x0", [T, D], F32, "ExternalInput")
    Wq = dram("Wq", [D, D], F32, "ExternalInput")
    Wk = dram("Wk", [D, D], F32, "ExternalInput")
    Wv = dram("Wv", [D, D], F32, "ExternalInput")
    Wo = dram("Wo", [D, D], F32, "ExternalInput")
    bqkvo = dram("bqkvo", [4, 1, D], F32, "ExternalInput")
    gamd = dram("gamd", [1, D], F32, "ExternalInput")
    betd = dram("betd", [1, D], F32, "ExternalInput")
    Wqkv = dram("Wqkv", [NL, D, 3 * D], F32, "ExternalInput")
    qscd = dram("qscd", [NL, 1, DH], F32, "ExternalInput")
    kscd = dram("kscd", [NL, 1, DH], F32, "ExternalInput")
    Woi = dram("Woi", [NL, D, D], F32, "ExternalInput")
    lngd = dram("lngd", [NL, 1, D], F32, "ExternalInput")
    lnbd = dram("lnbd", [NL, 1, D], F32, "ExternalInput")
    W1p = dram("W1p", [NL, D, 2 * DI], F32, "ExternalInput")
    b1pd = dram("b1pd", [NL, 1, 2 * DI], F32, "ExternalInput")
    W2d = dram("W2d", [NL, DI, D], F32, "ExternalInput")
    b2d = dram("b2d", [NL, 1, D], F32, "ExternalInput")
    idnd = dram("idnd", [128, 128], BF16, "ExternalInput")
    idnfd = dram("idnfd", [128, 128], F32, "ExternalInput")
    mtrid = dram("mtrid", [128, 3, 128], BF16, "ExternalInput")
    rotd = dram("rotd", [4, 128, NT, DH], F32, "ExternalInput")

    out = dram("out", [T, D], F32, "ExternalOutput")

    x1s = dram("x1s", [T, D], F32, "ExternalInput")
    zSTs = dram("zSTs", [NCH, 8, 128, ZCH], F32, "ExternalInput")
    oT1s = dram("oT1s", [16, 64, T], BF16, "ExternalInput")
    ktl = dram("ktl", [NL, 128, 8, 256], BF16, "ExternalInput")
    vtl = dram("vtl", [NL, 128, 2, H, 66], BF16, "ExternalInput")
    wcq = dram("wcq", [NL, D, 3 * D], BF16, "ExternalInput")
    wc1 = dram("wc1", [NL, D, 2 * DI], BF16, "ExternalInput")
    wc2 = dram("wc2", [NL, DI, D], BF16, "ExternalInput")
    wco = dram("wco", [NL, D, D], BF16, "ExternalInput")

    PA = kb.es.enter_context(nc.psum_tensor("PA", [128, 2048], F32))
    PB = kb.es.enter_context(nc.psum_tensor("PB", [128, 2048], F32))
    PBb = PB.bitcast(BF16)

    def bnk(i, n=512):
        t = PA if i < 4 else PB
        return t[:, (i % 4) * 512:(i % 4) * 512 + n]

    def pb_slots_b(nslots, width=128, base=0):
        return bass.AP(PBb, base * 1024, [[4096, 128], [1024, nslots], [1, width]])

    def pb_slots_f(nslots, width=128):
        return bass.AP(PB, 0, [[2048, 128], [512, nslots], [1, width]])

    sb = lambda scope, name, shape, dt=F32: scope.enter_context(
        nc.sbuf_tensor(name, list(shape), dt))
    es0 = kb.es
    idn = sb(es0, "idn", [128, 128], BF16)
    idnf = sb(es0, "idnf", [128, 128], F32)
    mtri = sb(es0, "mtri", [128, 3, 128], BF16)
    ones_row = sb(es0, "ones_row", [1, 512], BF16)
    ones128f = sb(es0, "ones128f", [1, 128], F32)
    ones64f = sb(es0, "ones64f", [1, 64], F32)
    onesv = sb(es0, "onesv", [128, 64], F32)
    epsc = sb(es0, "epsc", [128, 1], F32)
    ones_col = sb(es0, "ones_col", [128, 1], BF16)
    bbz = sb(es0, "bbz", [128, NL, 2, 4], F32)
    tinyA = sb(es0, "tinyA", [128, 16], F32)
    rowsc = sb(es0, "rowsc", [1, 512 * 4], F32)
    rowscb = sb(es0, "rowscb", [1, 1024], BF16)
    esB = contextlib.ExitStack()
    biasr = sb(esB, "biasr", [1, 4 * D], BF16)
    statx0 = sb(esB, "statx0", [128, NT, 2], F32)

    kb.dma("SP", idn[:, :], idnd[:, :], "dC")
    kb.dma("SP", idnf[:, :], idnfd[:, :], "dC")
    tok_const = kb.dma("SP", mtri[:, :, :], mtrid[:, :, :], "dC")
    kb.op("DVE", lambda v: v.memset(ones_row[:, :], 1.0))
    kb.op("DVE", lambda v: v.memset(ones128f[:, :], 1.0))
    kb.op("DVE", lambda v: v.memset(ones64f[:, :], 1.0))
    kb.op("DVE", lambda v: v.memset(onesv[:, :], 1.0))
    kb.op("DVE", lambda v: v.memset(epsc[:, :], EPS))
    kb.op("DVE", lambda v: v.memset(ones_col[:, :], 1.0))
    kb.op("DVE", lambda v: v.memset(bbz[:, :, :, :], 0.0), inc="sDVE")
    kb.op("DVE", lambda v: v.memset(tinyA[:, 14:16], 0.0))
    kb.act_tail = lambda a: a.activation(tinyA[:, 15:16], tinyA[:, 14:15], AF.Square)
    for i in range(4):
        kb.dma("POOL", biasr[0:1, i * D:(i + 1) * D], bqkvo[i, :, :], "dCW")
    tok_bias = ("dCW", kb.cnt["dCW"])

    es12 = contextlib.ExitStack()
    qT = sb(es12, "qT", [128, 2, T], BF16)
    kT = sb(es12, "kT", [128, 2, T], BF16)
    vh_o = sb(es12, "vh_o", [128, NT, 4, 66], BF16)
    oT64 = sb(es12, "oT64", [64, 4, T], BF16)
    aT = sb(es12, "aT", [128, 8, 512], BF16)
    xt2 = [sb(es12, "xt0", [128, D], F32), sb(es12, "xt1", [128, D], F32)]
    atile = sb(es12, "atile", [128, D], BF16)
    sqsc1 = sb(es12, "sqsc1", [128, D], F32)
    wqk_sl = sb(es12, "wqk_sl", [128, 8, 512], BF16)
    wv_sl = sb(es12, "wv_sl", [128, 8, 256], BF16)
    esb = [sb(es12, "esb%d" % i, [128, 512], BF16) for i in range(4)]
    ebuf = [sb(es12, "ebuf%d" % i, [128, 4, 128], BF16) for i in range(2)]
    mpart = sb(es12, "mpart", [128, 2, 8], F32)
    negm = sb(es12, "negm", [128, 2], F32)
    rlr = sb(es12, "rlr", [128, 256], F32)
    rbs = sb(es12, "rbs", [64, 256], F32)
    wstage = [sb(es12, "wstg0", [128, 1536], BF16), sb(es12, "wstg1", [128, 1536], BF16)]

    tok_vones = kb.op("DVE", lambda v: v.memset(vh_o[:, :, :, 64:65], 1.0), inc="sDVE")

    p0_jobs = []
    for l in range(NL):
        for (src, dst, rows, cols) in ((Wqkv, wcq, D, 3 * D), (W1p, wc1, D, 2 * DI),
                                       (W2d, wc2, DI, D), (Woi, wco, D, D)):
            for c in range((rows + 127) // 128):
                r0, r1 = c * 128, min((c + 1) * 128, rows)
                for h0 in range(0, cols, 1536):
                    w = min(1536, cols - h0)
                    p0_jobs.append((src[l, r0:r1, h0:h0 + w], dst[l, r0:r1, h0:h0 + w],
                                    r1 - r0, w))
    jobs_per_layer = len(p0_jobs) // NL
    p0 = {"i": 0, "tok": [None, None], "ltok": {}}

    def p0_pump(n):
        if P0MODE == "off":
            return
        for _ in range(n):
            i = p0["i"]
            if i >= len(p0_jobs):
                return
            p0["i"] += 1
            sap, dap, rows, cols = p0_jobs[i]
            stg = wstage[i % 2]
            kb.wait("POOL", p0["tok"][i % 2])
            t1 = kb.dma("POOL", stg[0:rows, 0:cols], sap, "d0l%d" % (i % 2))
            if P0MODE == "loadonly":
                p0["tok"][i % 2] = t1
                continue
            kb.wait("SP", t1)
            p0["tok"][i % 2] = kb.dma("SP", dap, stg[0:rows, 0:cols], "d0s%d" % (i % 2))
            if (i + 1) % jobs_per_layer == 0:
                p0["ltok"][(i + 1) // jobs_per_layer - 1] = list(p0["tok"])

    st = {"wfree": None, "vh": [None, None], "aTfree": [None, None],
          "x": [None, None], "at": None, "qk": [None, None], "vmm": None}

    def emit_P1(hq):
        f0 = hq * 256
        for s in range(min(NS, NSL)):
            kb.wait("POOL", st["wfree"])
            tok_w = None
            if "a" in P1DMA:
                kb.dma("POOL", wqk_sl[:, :, 0:256],
                       Wq[:, f0:f0 + 256].rearrange("(c p) n -> p c n", p=128), "dWa")
                kb.dma("POOL", wqk_sl[:, :, 256:512],
                       Wk[:, f0:f0 + 256].rearrange("(c p) n -> p c n", p=128), "dWb")
                tok_w = kb.dma("POOL", wv_sl[:, :, :],
                               Wv[:, f0:f0 + 256].rearrange("(c p) n -> p c n", p=128), "dWc")
            for tt in range(4):
                t = s * 4 + tt
                if "b" not in P1DMA:
                    continue
                kb.wait("SP", st["x"][tt % 2])
                tok_x = kb.dma("SP", xt2[tt % 2][:, :], x0[t * 128:(t + 1) * 128, :],
                               "dX1")
                xt = xt2[tt % 2]
                kb.wait("DVE", tok_x)
                if P1CUT >= 5:
                    st["x"][tt % 2] = tok_x
                    continue
                if hq == 0:
                    kb.wait("ACT", tok_x)
                    kb.wait("DVE", st.get("stats_prev"))
                    t_rs = kb.op("DVE", lambda v, xt=xt: v.reduce_sum(
                        tinyA[:, 0:1], xt[:, :], AX.X), inc="sDVE")
                    kb.wait("ACT", t_rs)
                    tsq = kb.op("ACT", lambda a, xt=xt: a.activation(
                        sqsc1[:, :], xt[:, :], AF.Square, accum_out=tinyA[:, 1:2]),
                        inc="sACT")
                    kb.wait("DVE", tsq)
                    kb.op("DVE", lambda v, t=t: v.tensor_scalar(
                        statx0[:, t, 0:1], tinyA[:, 0:1], 1.0 / D, None, ALU.mult))
                    kb.op("DVE", lambda v: v.tensor_scalar(
                        tinyA[:, 2:3], tinyA[:, 1:2], 1.0 / D, None, ALU.mult))
                    kb.op("DVE", lambda v, t=t: v.tensor_tensor(
                        tinyA[:, 3:4], statx0[:, t, 0:1], statx0[:, t, 0:1], ALU.mult))
                    tv = kb.op("DVE", lambda v: v.tensor_tensor(
                        tinyA[:, 4:5], tinyA[:, 2:3], tinyA[:, 3:4], ALU.subtract),
                        inc="sDVE")
                    kb.wait("ACT", tv)
                    kb.op("ACT", lambda a: a.activation(
                        tinyA[:, 5:6], tinyA[:, 4:5], AF.Ln, bias=epsc[:, 0:1]))
                    ts_ = kb.op("ACT", lambda a, t=t: a.activation(
                        statx0[:, t, 1:2], tinyA[:, 5:6], AF.Exp, scale=-0.5), inc="sACT")
                    kb.wait("DVE", ts_)
                    st["stats_prev"] = ts_
                if P1CUT >= 4:
                    st["x"][tt % 2] = tok_x
                    continue
                kb.wait("DVE", st["at"])
                tok_a = kb.op("DVE", lambda v, xt=xt, t=t: v.tensor_scalar(
                    atile[:, :], xt[:, :], statx0[:, t, 0:1], statx0[:, t, 1:2],
                    ALU.subtract, ALU.mult), inc="sDVE")
                st["x"][tt % 2] = tok_a
                if P1CUT >= 3:
                    continue
                kb.wait("PE", tok_a)
                kb.wait("PE", tok_const)
                for rnd in range(2):
                    kb.wait("PE", st["aTfree"][1])
                    for c4 in range(4):
                        c = rnd * 4 + c4
                        tok_pe = kb.op("PE", lambda p, c=c, c4=c4: p.matmul(
                            PBb[:, c4 * 1024:c4 * 1024 + 128],
                            atile[:, c * 128:(c + 1) * 128], idn[:, :],
                            is_transpose=True), inc="sPE")
                    kb.wait("DVE", tok_pe)
                    kb.wait("DVE", st["vmm"])
                    dst = bass.AP(aT, rnd * 4 * 512 + tt * 128,
                                  [[8 * 512, 128], [512, 4], [1, 128]])
                    st["aTfree"][1] = kb.op("DVE", lambda v, dst=dst: v.tensor_copy(
                        dst, pb_slots_b(4)), inc="sDVE")
                st["at"] = st["aTfree"][1]
            kb.wait("PE", tok_w)
            kb.wait("PE", tok_bias)
            kb.wait("PE", st["aTfree"][1])
            for f in range(PJF):
                bi = f % 2
                kb.wait("PE", st["qk"][bi])
                for kc in range(PJK):
                    kb.op("PE", lambda p, f=f, kc=kc, bi=bi: p.matmul(
                        bnk(bi), wqk_sl[:, kc, f * 128:(f + 1) * 128], aT[:, kc, :],
                        start=(kc == 0), stop=(kc == PJK - 1 and not PJB)))
                boff = (0 if f < 2 else D) + f0 + (f % 2) * 128
                if PJB:
                    tok_mm = kb.op("PE", lambda p, boff=boff, bi=bi: p.matmul(
                        bnk(bi), biasr[0:1, boff:boff + 128], ones_row[:, :],
                        start=False, stop=True), inc="sPE")
                else:
                    tok_mm = kb.pe_tok()
                if PJE:
                    kb.wait("ACT", tok_mm)
                    dstT = qT if f < 2 else kT
                    st["qk"][bi] = kb.op("ACT", lambda a, dstT=dstT, f=f, s=s, bi=bi: a.copy(
                        dstT[:, f % 2, s * 512:(s + 1) * 512], bnk(bi)), inc="sACT")
            if P1CUT >= 1:
                st["wfree"] = None
                continue
            for tt in range(4):
                t = s * 4 + tt
                bi = 2 + tt % 2
                kb.wait("PE", st["vh"][tt % 2])
                for kc in range(8):
                    kb.op("PE", lambda p, kc=kc, tt=tt, bi=bi: p.matmul(
                        bnk(bi, 256), aT[:, kc, tt * 128:(tt + 1) * 128], wv_sl[:, kc, :],
                        start=(kc == 0), stop=False))
                tok_mm = kb.op("PE", lambda p, bi=bi, f0=f0: p.matmul(
                    bnk(bi, 256), ones_row[0:1, 0:128],
                    biasr[0:1, 2 * D + f0:2 * D + f0 + 256],
                    start=False, stop=True), inc="sPE")
                st["vmm"] = tok_mm
                kb.wait("DVE", tok_mm)
                kb.wait("DVE", tok_vones)
                dst = bass.AP(vh_o, t * 4 * 66, [[NT * 4 * 66, 128], [66, 4], [1, 64]])
                st["vh"][tt % 2] = kb.op("DVE", lambda v, bi=bi, dst=dst: v.tensor_copy(
                    dst, bnk(bi, 256).rearrange("p (h d) -> p h d", h=4)), inc="sDVE")
            st["wfree"] = st["vmm"]
            p0_pump(3)

    def emit_P2(hq):
        st2 = {"exp": [None, None], "red": [None, None], "eb": [None, None],
               "norm": None, "av": None, "rb": None}
        for pr in range(2):
            for qt in range(NT):
                Kend = (qt + 1) * 128
                nkb = (Kend + 511) // 512
                for kbi in range(nkb):
                    N = min(512, Kend - kbi * 512)
                    bA = (kbi % 2) * 2
                    kb.wait("PE", st2["red"][kbi % 2])
                    if kbi <= 1:
                        kb.wait("PE", st2["norm"])
                        kb.wait("PE", st2["exp"][0])
                        kb.wait("PE", st2["exp"][1])
                        kb.wait("PE", st2["av"])
                    kb.op("PE", lambda p, pr=pr, qt=qt, kbi=kbi, N=N, bA=bA: p.matmul(
                        bnk(bA, N), qT[0:64, pr, qt * 128:(qt + 1) * 128],
                        kT[0:64, pr, kbi * 512:kbi * 512 + N],
                        start=True, stop=True, tile_position=(0, 0)))
                    mB = kb.op("PE", lambda p, pr=pr, qt=qt, kbi=kbi, N=N, bA=bA: p.matmul(
                        bnk(bA + 1, N), qT[64:128, pr, qt * 128:(qt + 1) * 128],
                        kT[64:128, pr, kbi * 512:kbi * 512 + N],
                        start=True, stop=True, tile_position=(64, 0)), inc="sPE")
                    kb.wait("DVE", mB)
                    if kbi == 0:
                        kb.wait("DVE", st2["exp"][0])
                        kb.wait("DVE", st2["exp"][1])
                    src = bass.AP(PA, bA * 512, [[2048, 128], [512, 2], [1, N]])
                    st2["red"][kbi % 2] = kb.op("DVE", lambda v, kbi=kbi, src=src:
                                                v.reduce_max(mpart[:, :, kbi:kbi + 1],
                                                             src, AX.X), inc="sDVE")
                kb.op("DVE", lambda v, nkb=nkb: v.reduce_max(
                    negm[:, 0:2], mpart[:, :, 0:nkb], AX.X, negate=True))
                tn2 = kb.op("DVE", lambda v: v.tensor_scalar(
                    negm[:, 0:2], negm[:, 0:2], 0.125, None, ALU.mult), inc="sDVE")
                kb.wait("ACT", tn2)
                oA = PA[0:65, 1024:1152]
                oB = PA[0:65, 1536:1664]
                for kbi in range(nkb):
                    N = min(512, Kend - kbi * 512)
                    nch = N // 128
                    kb.wait("PE", st2["exp"][0])
                    kb.wait("PE", st2["exp"][1])
                    if kbi == 0:
                        kb.wait("PE", st2["red"][0])
                        kb.wait("PE", st2["red"][1])
                        kb.wait("PE", st2["norm"])
                    kb.op("PE", lambda p, pr=pr, qt=qt, kbi=kbi, N=N: p.matmul(
                        bnk(0, N), qT[0:64, pr, qt * 128:(qt + 1) * 128],
                        kT[0:64, pr, kbi * 512:kbi * 512 + N],
                        start=True, stop=True, tile_position=(0, 0)))
                    mB = kb.op("PE", lambda p, pr=pr, qt=qt, kbi=kbi, N=N: p.matmul(
                        bnk(1, N), qT[64:128, pr, qt * 128:(qt + 1) * 128],
                        kT[64:128, pr, kbi * 512:kbi * 512 + N],
                        start=True, stop=True, tile_position=(64, 0)), inc="sPE")
                    kb.wait("ACT", mB)
                    eA, eB = esb[2 * (kbi % 2)], esb[2 * (kbi % 2) + 1]
                    kb.wait("ACT", st2["eb"][0])
                    kb.wait("ACT", st2["eb"][1])
                    st2["exp"][0] = kb.op("ACT", lambda a, eA=eA, N=N: a.activation(
                        eA[:, 0:N], bnk(0, N), AF.Exp, bias=negm[:, 0:1], scale=0.125),
                        inc="sACT")
                    st2["exp"][1] = kb.op("ACT", lambda a, eB=eB, N=N: a.activation(
                        eB[:, 0:N], bnk(1, N), AF.Exp, bias=negm[:, 1:2], scale=0.125),
                        inc="sACT")
                    for hc in range(2):
                        etile = eA if hc == 0 else eB
                        obank = oA if hc == 0 else oB
                        ebf = ebuf[hc]
                        kb.wait("PE", st2["exp"][hc])
                        kb.wait("PE", st2["eb"][0])
                        kb.wait("PE", st2["eb"][1])
                        for cc in range(nch):
                            tok_t = kb.op("PE", lambda p, etile=etile, cc=cc: p.matmul(
                                PBb[:, cc * 1024:cc * 1024 + 128],
                                etile[:, cc * 128:(cc + 1) * 128], idn[:, :],
                                is_transpose=True), inc="sPE")
                        kb.wait("DVE", tok_t)
                        if kbi == nkb - 1:
                            if nch > 1:
                                s2 = pb_slots_b(nch - 1)
                                d2 = bass.AP(ebf, 0, [[512, 128], [128, nch - 1], [1, 128]])
                                kb.op("DVE", lambda v, s2=s2, d2=d2: v.tensor_copy(d2, s2))
                            sm = pb_slots_b(1, base=nch - 1)
                            dm = bass.AP(ebf, (nch - 1) * 128,
                                         [[512, 128], [128, 1], [1, 128]])
                            st2["eb"][hc] = kb.op("DVE", lambda v, sm=sm, dm=dm:
                                                  v.tensor_tensor(dm, sm, mtri[:, 2:3, :],
                                                                  ALU.mult), inc="sDVE")
                        else:
                            s2 = pb_slots_b(nch)
                            d2 = bass.AP(ebf, 0, [[512, 128], [128, nch], [1, 128]])
                            st2["eb"][hc] = kb.op("DVE", lambda v, s2=s2, d2=d2:
                                                  v.tensor_copy(d2, s2), inc="sDVE")
                        kb.wait("PE", st2["eb"][hc])
                        h4 = 2 * pr + hc
                        for cc in range(nch):
                            st2["av"] = kb.op("PE", lambda p, kt_t=kbi * 4 + cc, h4=h4,
                                              ebf=ebf, cc=cc, obank=obank,
                                              first=(kbi == 0 and cc == 0),
                                              last=(kbi == nkb - 1 and cc == nch - 1):
                                              p.matmul(obank, vh_o[:, kt_t, h4, 0:65],
                                                       ebf[:, cc, :],
                                                       start=first, stop=last), inc="sPE")
                kb.wait("DVE", st2["av"])
                kb.wait("DVE", st2["rb"])
                rl_src = bass.AP(PA, 64 * 2048 + 1024, [[2048, 1], [512, 2], [1, 128]])
                rl_dst = bass.AP(rlr, 64 * 256, [[256, 1], [128, 2], [1, 128]])
                t_rl = kb.op("DVE", lambda v: v.reciprocal(rl_dst, rl_src), inc="sDVE")
                kb.wait("PE", t_rl)
                t_rb = kb.op("PE", lambda p: p.matmul(
                    PB[0:64, 0:256], onesv[64:65, 0:64], rlr[64:65, 0:256],
                    start=True, stop=True, tile_position=(64, 0)), inc="sPE")
                st2["rb"] = t_rb
                kb.wait("DVE", t_rb)
                t_rbs = kb.op("DVE", lambda v: v.tensor_copy(rbs[:, :], PB[0:64, 0:256]),
                              inc="sDVE")
                kb.wait("DVE", t_rbs)
                o_src = bass.AP(PA, 1024, [[2048, 64], [512, 2], [1, 128]])
                o_dst = bass.AP(oT64, (2 * pr) * T + qt * 128,
                                [[4 * T, 64], [T, 2], [1, 128]])
                rb_src = bass.AP(rbs, 0, [[256, 64], [128, 2], [1, 128]])
                st2["norm"] = kb.op("DVE", lambda v, o_dst=o_dst, o_src=o_src,
                                    rb_src=rb_src: v.tensor_tensor(
                    o_dst, o_src, rb_src, ALU.mult), inc="sDVE")
            p0_pump(4)
        kb.barrier()
        for j in range(4):
            kb.dma("SP", oT1s[hq * 4 + j, :, :], oT64[0:64, j, :], "dOsp")
        kb.barrier()

    for hq in range(min(4, HQN)):
        if "1" in phases:
            emit_P1(hq)
            kb.barrier()
        if "2" in phases:
            emit_P2(hq)
    if "4" in phases:
        p0_pump(len(p0_jobs))
    kb.barrier()
    if os.environ.get("K_DBG", "") == "qt":
        kb.dma("POOL", out[0:256, :].rearrange("(p c) t -> p c t", p=128),
               qT[:, :, :], "dDbgq")
        kb.dma("POOL", out[256:512, :].rearrange("(p c) t -> p c t", p=128),
               kT[:, :, :], "dDbgk")
        kb.dma("POOL", out[512:640, :].rearrange("p (c n) -> p c n", c=2),
               aT[:, 0:2, :], "dDbga")
        kb.dma("SP", out[640:768, 0:2 * NT].rearrange("p (t c) -> p t c", c=2),
               statx0[:, :, :], "dDbgs")
    es12.close()

    es3 = contextlib.ExitStack()
    wo64 = sb(es3, "wo64", [64, 16, D], BF16)
    gbbc = sb(es3, "gbbc", [128, 2 * D], F32)
    gbrow = sb(es3, "gbrow", [1, 2 * D], F32)
    oTh = [sb(es3, "oTha", [64, 16, 512], BF16), sb(es3, "oThb", [64, 16, 512], BF16)]
    xt3 = [sb(es3, "xt3a", [128, D], F32), sb(es3, "xt3b", [128, D], F32)]
    ytile = sb(es3, "ytile", [128, D], F32)
    t1s = sb(es3, "t1s", [128, D], F32)
    yt2 = sb(es3, "yt2", [128, D], F32)
    x1t = [sb(es3, "x1ta", [128, D], F32), sb(es3, "x1tb", [128, D], F32)]
    ztok3 = sb(es3, "ztok3", [128, D], F32)
    zstg = [sb(es3, "zstga", [128, 8, 128], F32), sb(es3, "zstgb", [128, 8, 128], F32)]
    sqsc3 = sb(es3, "sqsc3", [128, D], F32)

    tok_wo = kb.dma("POOL", wo64[0:64, :, :],
                    Wo.rearrange("(j p) n -> p j n", p=64), "dW3")
    t_g = kb.dma("SP", gbrow[0:1, 0:D], gamd[:, :], "dX3g")
    t_b = kb.dma("SP", gbrow[0:1, D:2 * D], betd[:, :], "dX3g")
    kb.wait("PE", t_g)
    kb.wait("PE", t_b)
    for i in range(4):
        t_mm = kb.op("PE", lambda p, i=i: p.matmul(
            bnk(4 + i), ones128f[:, :], gbrow[0:1, i * 512:(i + 1) * 512],
            start=True, stop=True), inc="sPE")
    kb.wait("DVE", t_mm)
    for i in range(4):
        t_gb = kb.op("DVE", lambda v, i=i: v.tensor_copy(
            gbbc[:, i * 512:(i + 1) * 512], bnk(4 + i)), inc="sDVE")
    kb.wait("PE", t_gb)
    kb.wait("PE", tok_wo)

    def p3_rstd(c_in, c_out):
        kb.op("DVE", lambda v: v.tensor_scalar(
            tinyA[:, c_out - 1:c_out], tinyA[:, c_in:c_in + 1], 1.0 / D, None, ALU.mult))
        kb.op("DVE", lambda v: v.tensor_scalar(
            tinyA[:, 12:13], tinyA[:, c_in + 1:c_in + 2], 1.0 / D, None, ALU.mult))
        kb.op("DVE", lambda v: v.tensor_tensor(
            tinyA[:, 13:14], tinyA[:, c_out - 1:c_out], tinyA[:, c_out - 1:c_out],
            ALU.mult))
        tv = kb.op("DVE", lambda v: v.tensor_tensor(
            tinyA[:, 14:15], tinyA[:, 12:13], tinyA[:, 13:14], ALU.subtract), inc="sDVE")
        kb.wait("ACT", tv)
        kb.op("ACT", lambda a: a.activation(tinyA[:, 15:16], tinyA[:, 14:15], AF.Ln,
                                            bias=epsc[:, 0:1]))
        tr = kb.op("ACT", lambda a, c_out=c_out: a.activation(
            tinyA[:, c_out:c_out + 1], tinyA[:, 15:16], AF.Exp, scale=-0.5), inc="sACT")
        kb.wait("DVE", tr)

    def emit_P3():
        tok_oth = [None, None]
        tok_x0l = [None, None]
        tok_x1d = [None, None]
        tok_zst = [None, None]
        tok_y = [None, None]
        tok_ev = None
        for t in range(NT):
            s, tt = t // 4, t % 4
            if tt == 0:
                kb.wait("SP", tok_oth[s % 2])
                tok_o = kb.dma("SP", oTh[s % 2][0:64, :, :],
                               oT1s[:, :, s * 512:(s + 1) * 512].rearrange(
                                   "j p n -> p j n"), "dX3o")
                tok_oth[s % 2] = tok_o
                kb.wait("PE", tok_o)
            kb.wait("SP", tok_x0l[t % 2])
            tok_x = kb.dma("SP", xt3[t % 2][:, :], x0[t * 128:(t + 1) * 128, :], "dX3x")
            for fb in range(2):
                kb.wait("PE", tok_y[fb])
                for j in range(16):
                    kb.op("PE", lambda p, fb=fb, j=j, sm=s % 2, tt=tt: p.matmul(
                        bnk(fb), oTh[sm][0:64, j, tt * 128:(tt + 1) * 128],
                        wo64[0:64, j, fb * 512:(fb + 1) * 512],
                        start=(j == 0), stop=False))
                tok_mm = kb.op("PE", lambda p, fb=fb: p.matmul(
                    bnk(fb), ones_row[0:1, 0:128],
                    biasr[0:1, 3 * D + fb * 512:3 * D + fb * 512 + 512],
                    start=False, stop=True), inc="sPE")
                kb.wait("DVE", tok_mm)
                tok_y[fb] = kb.op("DVE", lambda v, fb=fb: v.tensor_scalar(
                    ytile[:, fb * 512:(fb + 1) * 512], bnk(fb), 1.0, None, ALU.mult,
                    ALU.add, accum_out=tinyA[:, fb:fb + 1]), inc="sDVE")
            ty2 = kb.op("DVE", lambda v: v.tensor_tensor(
                tinyA[:, 2:3], tinyA[:, 0:1], tinyA[:, 1:2], ALU.add), inc="sDVE")
            kb.wait("ACT", ty2)
            tsq1 = kb.op("ACT", lambda a: a.activation(
                sqsc3[:, :], ytile[:, :], AF.Square, accum_out=tinyA[:, 3:4]),
                inc="sACT")
            kb.wait("DVE", tsq1)
            p3_rstd(2, 5)
            kb.op("DVE", lambda v: v.tensor_scalar(
                t1s[:, :], ytile[:, :], tinyA[:, 4:5], tinyA[:, 5:6],
                ALU.subtract, ALU.mult))
            kb.op("DVE", lambda v: v.tensor_tensor(
                yt2[:, :], t1s[:, :], gbbc[:, 0:D], ALU.mult))
            t_yt = kb.op("DVE", lambda v: v.scalar_tensor_tensor(
                t1s[:, :], yt2[:, :], 1.0, gbbc[:, D:2 * D], ALU.mult, ALU.add,
                accum_out=tinyA[:, 6:7]), inc="sDVE")
            kb.wait("ACT", t_yt)
            tsq2 = kb.op("ACT", lambda a: a.activation(
                sqsc3[:, :], t1s[:, :], AF.Square, accum_out=tinyA[:, 7:8]),
                inc="sACT")
            kb.wait("DVE", tsq2)
            p3_rstd(6, 9)
            kb.wait("DVE", tok_x)
            kb.wait("DVE", tok_x1d[t % 2])
            kb.op("DVE", lambda v: v.tensor_scalar(
                yt2[:, :], t1s[:, :], tinyA[:, 8:9], tinyA[:, 9:10],
                ALU.subtract, ALU.mult))
            tok_x1 = kb.op("DVE", lambda v, t=t: v.scalar_tensor_tensor(
                x1t[t % 2][:, :], yt2[:, :], 1.0, xt3[t % 2][:, :], ALU.mult, ALU.add,
                accum_out=tinyA[:, 0:1]), inc="sDVE")
            tok_x0l[t % 2] = tok_x1
            kb.wait("SP", tok_x1)
            tok_x1d[t % 2] = kb.dma("SP", x1s[t * 128:(t + 1) * 128, :],
                                    x1t[t % 2][:, :], "dO3x")
            kb.wait("ACT", tok_x1)
            tsq3 = kb.op("ACT", lambda a, t=t: a.activation(
                sqsc3[:, :], x1t[t % 2][:, :], AF.Square, accum_out=tinyA[:, 1:2]),
                inc="sACT")
            kb.wait("DVE", tsq3)
            p3_rstd(0, 11)
            tok_zt = kb.op("DVE", lambda v, t=t: v.tensor_scalar(
                ztok3[:, :], x1t[t % 2][:, :], tinyA[:, 10:11], tinyA[:, 11:12],
                ALU.subtract, ALU.mult), inc="sDVE")
            kb.wait("PE", tok_zt)
            for rnd in range(2):
                kb.wait("PE", tok_zst[t % 2] if rnd == 0 else tok_ev)
                for c4 in range(4):
                    c = rnd * 4 + c4
                    tok_pe = kb.op("PE", lambda p, c=c, c4=c4: p.matmul(
                        PB[:, c4 * 512:c4 * 512 + 128], ztok3[:, c * 128:(c + 1) * 128],
                        idnf[:, :], is_transpose=True), inc="sPE")
                kb.wait("DVE", tok_pe)
                dst = bass.AP(zstg[t % 2], rnd * 4 * 128,
                              [[8 * 128, 128], [128, 4], [1, 128]])
                tok_ev = kb.op("DVE", lambda v, dst=dst: v.tensor_copy(
                    dst, pb_slots_f(4)), inc="sDVE")
            kb.wait("SP", tok_ev)
            q, tq = t // ZNT, t % ZNT
            tok_zd = kb.dma("SP",
                            zSTs[q, :, :, tq * 128:(tq + 1) * 128].rearrange(
                                "c p n -> p c n"),
                            zstg[t % 2][:, :, :], "dO3z%d" % (t % 2))
            tok_zst[t % 2] = tok_zd

    if "3" in phases:
        emit_P3()
    kb.barrier()
    es3.close()
    esB.close()

    es4 = contextlib.ExitStack()
    z = sb(es4, "z", [128, 8, ZCH + 1], F32)
    zb = sb(es4, "zb", [128, 8, 513], BF16)
    zb2t = [sb(es4, "zb2a", [128, 513], BF16), sb(es4, "zb2b", [128, 513], BF16)]
    qkst = sb(es4, "qkst", [128, 4, 2048], BF16)
    tmpA = sb(es4, "tmpA", [128, 2048], BF16)
    tmpB = sb(es4, "tmpB", [128, 2048], BF16)
    sqsc4 = sb(es4, "sqsc4", [128, 512], F32)
    ssq = sb(es4, "ssq", [128, 4, 32], F32)
    rsd = sb(es4, "rsd", [128, 4, 32], F32)
    qTt = sb(es4, "qTt", [128, 4, 8, 128], BF16)
    kwin = [sb(es4, "kwin0", [128, 8, 384], BF16), sb(es4, "kwin1", [128, 8, 384], BF16)]
    vh_i = sb(es4, "vh_i", [128, 6, H, 66], BF16)
    oTi = sb(es4, "oTi", [64, 16, 256], BF16)
    xn = sb(es4, "xn", [128, 8, 512], BF16)
    hT = sb(es4, "hT", [128, 22, 512], BF16)
    eff = sb(es4, "eff", [128, 4, ZNT, DH], BF16)
    tbuf4 = sb(es4, "tbuf4", [128, ZNT, DH], F32)
    qs8bc = sb(es4, "qs8bc", [128, DH], F32)
    kscbc = sb(es4, "kscbc", [128, DH], F32)
    qkv_sl = [sb(es4, "qkvsl0", [128, 8, 512], BF16), sb(es4, "qkvsl1", [128, 8, 512], BF16)]
    w1_sl = [sb(es4, "w1sl0", [128, 8, 256], BF16), sb(es4, "w1sl1", [128, 8, 256], BF16)]
    w2_sl = [sb(es4, "w2sl0", [128, 22, 128], BF16), sb(es4, "w2sl1", [128, 22, 128], BF16)]
    woi_sl = [sb(es4, "woisl0", [64, 16, 128], BF16), sb(es4, "woisl1", [64, 16, 128], BF16)]
    b1st = sb(es4, "b1st", [64, 128], F32)
    b1col = sb(es4, "b1col", [128, 44], F32)
    b2st = sb(es4, "b2st", [8, 128], F32)
    b2col = sb(es4, "b2col", [128, 8], F32)
    lngb = sb(es4, "lngb", [1, D], BF16)
    lnbb = sb(es4, "lnbb", [1, D], BF16)
    qrow = sb(es4, "qrow", [1, 2 * DH], F32)
    aas = sb(es4, "aas", [128, 512], BF16)
    ggs = sb(es4, "ggs", [128, 512], BF16)
    aas2 = sb(es4, "aas2", [128, 512], BF16)
    ggs2 = sb(es4, "ggs2", [128, 512], BF16)
    esb4 = [sb(es4, "esb4%d" % i, [128, 512], BF16) for i in range(2)]
    ebuf4 = [sb(es4, "ebuf4%d" % i, [128, 3, 128], BF16) for i in range(2)]
    negm4 = sb(es4, "negm4", [128, 2], F32)
    rlr4 = sb(es4, "rlr4", [128, 256], F32)
    rbs4 = sb(es4, "rbs4", [64, 256], F32)
    zvt = sb(es4, "zvt", [128, 512], F32)

    kb.op("DVE", lambda v: v.memset(vh_i[:, :, :, :], 1.0))
    kb.op("DVE", lambda v: v.memset(b1st[:, :], 0.0))
    tok_vi = kb.op("DVE", lambda v: v.memset(kwin[0][:, :, :], 0.0), inc="sDVE")
    tok_vi2 = kb.op("DVE", lambda v: v.memset(kwin[1][:, :, :], 0.0), inc="sDVE")
    kb.wait("PE", tok_vi)
    kb.wait("PE", tok_vi2)
    kb.wait("POOL", tok_vi2)
    kb.wait("SP", tok_vi2)

    p4 = {"zst": None, "slab": [None, None], "w1": [None, None], "w2": [None, None],
          "woi": [None, None], "qk_ev": [None, None], "vh_ev": [None] * 6,
          "rot": [None, None], "kwin": [None, None], "exp": [None, None],
          "eb": [None, None], "av": None, "norm": None, "oTi_used": None,
          "zupd": None, "hTf": [None] * NP, "xnf": None, "tbl": None,
          "b1": None, "stats": None, "rb": None, "sq": [None, None], "w1ga": [None, None],
          "zb_used": None, "bb": None}

    def tgl(q, s, tt):
        return q * ZNT + s * 4 + tt

    def p4_tables(L, q):
        t_q = kb.dma("SP", qrow[0:1, 0:DH], qscd[L, :, :], "dXq")
        t_k = kb.dma("SP", qrow[0:1, DH:2 * DH], kscd[L, :, :], "dXq")
        kb.wait("PE", t_q)
        kb.wait("PE", t_k)
        kb.wait("PE", p4["tbl"])
        kb.op("PE", lambda p: p.matmul(PB[0:128, 0:64], ones128f[:, :],
                                       qrow[0:1, 0:DH], start=True, stop=True))
        m2 = kb.op("PE", lambda p: p.matmul(PB[0:128, 512:576], ones128f[:, :],
                                            qrow[0:1, DH:2 * DH], start=True, stop=True),
                   inc="sPE")
        kb.wait("DVE", m2)
        kb.op("DVE", lambda v: v.tensor_scalar(qs8bc[:, :], PB[0:128, 0:64], 8.0, None,
                                               ALU.mult))
        t_bc = kb.op("DVE", lambda v: v.tensor_copy(kscbc[:, :], PB[0:128, 512:576]),
                     inc="sDVE")
        kb.wait("PE", t_bc)
        tok_prev = t_bc
        for ti, bc in ((0, qs8bc), (1, qs8bc), (2, kscbc), (3, kscbc)):
            kb.wait("SP", tok_prev)
            tok_t = kb.dma("SP", tbuf4[:, :, :],
                           rotd[ti, :, q * ZNT:(q + 1) * ZNT, :], "dXt")
            kb.wait("DVE", tok_t)
            dstv = bass.AP(eff, ti * ZNT * DH, [[4 * ZNT * DH, 128], [DH, ZNT], [1, DH]])
            if ti % 2 == 0:
                src0 = bass.AP(tbuf4, 0, [[ZNT * DH, 128], [DH, ZNT], [1, DH]])
                src1 = bass.AP(bc, 0, [[DH, 128], [0, ZNT], [1, DH]])
                tok_prev = kb.op("DVE", lambda v, dstv=dstv, src0=src0, src1=src1:
                                 v.tensor_tensor(dstv, src0, src1, ALU.mult), inc="sDVE")
            else:
                for lohi in range(2):
                    s0 = bass.AP(tbuf4, lohi * 32, [[ZNT * DH, 128], [DH, ZNT], [1, 32]])
                    s1 = bass.AP(bc, (1 - lohi) * 32, [[DH, 128], [0, ZNT], [1, 32]])
                    dv = bass.AP(eff, ti * ZNT * DH + lohi * 32,
                                 [[4 * ZNT * DH, 128], [DH, ZNT], [1, 32]])
                    tok_prev = kb.op("DVE", lambda v, dv=dv, s0=s0, s1=s1:
                                     v.tensor_tensor(dv, s0, s1, ALU.mult), inc="sDVE")
        p4["tbl"] = tok_prev
        kb.wait("DVE", p4["tbl"])

    def p4_zb_convert(L, stage, s):
        kb.wait("ACT", p4["zupd"])
        kb.wait("ACT", p4["zb_used"])
        for c in range(8):
            tok_c = kb.op("ACT", lambda a, c=c, s=s: a.copy(
                zb[:, c, 0:513], z[:, c, s * 512:s * 512 + 513]), inc="sACT")
        for c in range(4):
            tok_c = kb.op("ACT", lambda a, c=c, L=L, stage=stage: a.copy(
                zb[:, c + 4, 0:1], bbz[:, L, stage, c:c + 1]), inc="sACT")
        kb.wait("DVE", tok_c)
        kb.wait("DVE", p4["bb"])
        tok_s = kb.op("DVE", lambda v, L=L, stage=stage, s=s: v.tensor_copy(
            bbz[:, L, stage, 0:4],
            z[:, 4:8, s * 512 + 512:s * 512 + 513].rearrange("p c o -> p (c o)")),
            inc="sDVE")
        p4["bb"] = tok_s
        return tok_c, tok_s

    def p4_attn_stage(L, q):
        for s in range(ZNS):
            tok_zbc, tok_bbs = p4_zb_convert(L, 0, s)
            for fb in range(6):
                sl = qkv_sl[fb % 2]
                kb.wait("POOL", p4["slab"][fb % 2])
                tok_sl = kb.dma("POOL", sl[:, :, :],
                                wcq[L, :, fb * 512:(fb + 1) * 512].rearrange(
                                    "(c p) n -> p c n", p=128), "dWq4")
                kb.wait("PE", tok_sl)
                kb.wait("PE", tok_zbc)
                for tt in range(4):
                    bi = tt % 2
                    kb.wait("PE", p4["qk_ev"][bi])
                    for kc in range(8):
                        off = 1 if kc < 4 else 0
                        kb.op("PE", lambda p, kc=kc, tt=tt, bi=bi, off=off, sl=sl:
                              p.matmul(bnk(bi),
                                       zb[:, kc, off + tt * 128:off + tt * 128 + 128],
                                       sl[:, kc, :], start=(kc == 0), stop=(kc == 7)))
                    tok_mm = kb.pe_tok()
                    if fb < 4:
                        kb.wait("ACT", tok_mm)
                        kb.wait("ACT", p4["sq"][bi])
                        kb.wait("ACT", p4.get("qkst_rd%d" % tt))
                        kb.op("ACT", lambda a, tt=tt, fb=fb, bi=bi: a.copy(
                            qkst[:, tt, fb * 512:(fb + 1) * 512], bnk(bi)))
                        t_sq = kb.op("ACT", lambda a, bi=bi: a.activation(
                            sqsc4[:, :], bnk(bi), AF.Square), inc="sACT")
                        kb.wait("DVE", t_sq)
                        t_rd = kb.op("DVE", lambda v, tt=tt, fb=fb: v.reduce_sum(
                            ssq[:, tt, fb * 8:fb * 8 + 8],
                            sqsc4[:, :].rearrange("p (h d) -> p h d", h=8), AX.X),
                            inc="sDVE")
                        p4["qk_ev"][bi] = t_rd
                        p4["sq"][bi] = t_rd
                    else:
                        tg = tgl(q, s, tt)
                        slot = tg % 6
                        kb.wait("DVE", tok_mm)
                        kb.wait("DVE", p4["vh_ev"][slot])
                        dstv = bass.AP(vh_i, slot * H * 66 + (fb - 4) * 8 * 66,
                                       [[6 * H * 66, 128], [66, 8], [1, 64]])
                        t_ev = kb.op("DVE", lambda v, bi=bi, dstv=dstv: v.tensor_copy(
                            dstv, bnk(bi).rearrange("p (h d) -> p h d", h=8)),
                            inc="sDVE")
                        p4["vh_ev"][slot] = t_ev
                        p4["qk_ev"][bi] = t_ev
                p4["slab"][fb % 2] = p4["qk_ev"][1]
            kb.wait("DVE", p4["qk_ev"][0])
            kb.wait("DVE", p4["qk_ev"][1])
            tmx = kb.op("DVE", lambda v: v.tensor_scalar(
                rsd[:, :, :], ssq[:, :, :], 1e-24, None, ALU.max), inc="sDVE")
            kb.wait("ACT", tmx)
            kb.op("ACT", lambda a: a.activation(rsd[:, :, :], rsd[:, :, :], AF.Ln))
            trs = kb.op("ACT", lambda a: a.activation(
                rsd[:, :, :], rsd[:, :, :], AF.Exp, scale=-0.5), inc="sACT")
            for tt in range(4):
                tg = tgl(q, s, tt)
                ti_c = s * 4 + tt
                cur, prv = tg % 2, 1 - tg % 2
                kb.wait("DVE", p4["rot"][tt % 2])
                kb.wait("DVE", p4.get("tmpA_free"))
                kb.wait("DVE", p4.get("c2"))
                kb.wait("DVE", trs)
                for half, tbl in ((0, 0), (1, 2)):
                    qoff = half * 1024
                    in0 = bass.AP(qkst, tt * 2048 + qoff,
                                  [[4 * 2048, 128], [64, 16], [1, 64]])
                    ce = bass.AP(eff, (tbl * ZNT + ti_c) * DH,
                                 [[4 * ZNT * DH, 128], [0, 16], [1, 64]])
                    o0 = bass.AP(tmpA, qoff, [[2048, 128], [64, 16], [1, 64]])
                    kb.op("DVE", lambda v, o0=o0, in0=in0, ce=ce: v.tensor_tensor(
                        o0, in0, ce, ALU.mult))
                    se_lo = bass.AP(eff, ((tbl + 1) * ZNT + ti_c) * DH,
                                    [[4 * ZNT * DH, 128], [0, 16], [1, 32]])
                    se_hi = bass.AP(eff, ((tbl + 1) * ZNT + ti_c) * DH + 32,
                                    [[4 * ZNT * DH, 128], [0, 16], [1, 32]])
                    i_hi = bass.AP(qkst, tt * 2048 + qoff + 32,
                                   [[4 * 2048, 128], [64, 16], [1, 32]])
                    i_lo = bass.AP(qkst, tt * 2048 + qoff,
                                   [[4 * 2048, 128], [64, 16], [1, 32]])
                    oB_lo = bass.AP(tmpB, qoff, [[2048, 128], [64, 16], [1, 32]])
                    oB_hi = bass.AP(tmpB, qoff + 32, [[2048, 128], [64, 16], [1, 32]])
                    kb.op("DVE", lambda v, a1=oB_lo, a2=i_hi, a3=se_lo: v.tensor_tensor(
                        a1, a2, a3, ALU.mult))
                    tok_r = kb.op("DVE", lambda v, a1=oB_hi, a2=i_lo, a3=se_hi:
                                  v.tensor_tensor(a1, a2, a3, ALU.mult), inc="sDVE")
                kb.wait("POOL", tok_r)
                kb.wait("POOL", p4.get("qkst_rd%d" % tt))
                tok_c2 = kb.op("POOL", lambda g, tt=tt: g.tensor_tensor(
                    qkst[:, tt, :], tmpA[:, :], tmpB[:, :], ALU.add), inc="sPOOL")
                p4["c2"] = tok_c2
                kb.wait("DVE", tok_c2)
                rsv = bass.AP(rsd, tt * 32, [[4 * 32, 128], [1, 32], [0, 64]])
                tok_d = kb.op("DVE", lambda v, tt=tt, rsv=rsv: v.tensor_tensor(
                    tmpA[:, :], qkst[:, tt, :], rsv, ALU.mult), inc="sDVE")
                p4["rot"][tt % 2] = tok_d
                p4["qkst_rd%d" % tt] = tok_d
                kb.wait("POOL", p4.get("kwinw%d" % prv))
                kb.wait("POOL", p4["av"])
                if tg > 0:
                    tok_sl2 = kb.op("POOL", lambda g, cur=cur, prv=prv: g.tensor_copy(
                        kwin[cur][:, :, 0:256], kwin[prv][:, :, 128:384]), inc="sPOOL")
                else:
                    tok_sl2 = kb.op("POOL", lambda g, cur=cur: g.memset(
                        kwin[cur][:, :, 0:256], 0.0), inc="sPOOL")
                kb.wait("PE", tok_d)
                tok_te = None
                for rnd in range(4):
                    kb.wait("PE", p4["eb"][0])
                    kb.wait("PE", p4["eb"][1])
                    kb.wait("PE", tok_te)
                    for c4 in range(4):
                        c = rnd * 4 + c4
                        tok_pe = kb.op("PE", lambda p, c=c, c4=c4: p.matmul(
                            PBb[:, c4 * 1024:c4 * 1024 + 128],
                            tmpA[:, c * 128:(c + 1) * 128], idn[:, :],
                            is_transpose=True), inc="sPE")
                    kb.wait("DVE", tok_pe)
                    if rnd < 2:
                        dstv = bass.AP(qTt, tt * 8 * 128 + rnd * 4 * 128,
                                       [[4 * 8 * 128, 128], [128, 4], [1, 128]])
                    else:
                        kb.wait("DVE", tok_sl2)
                        dstv = bass.AP(kwin[cur], (rnd - 2) * 4 * 384 + 256,
                                       [[8 * 384, 128], [384, 4], [1, 128]])
                    tok_te = kb.op("DVE", lambda v, dstv=dstv: v.tensor_copy(
                        dstv, pb_slots_b(4)), inc="sDVE")
                p4["tmpA_free"] = tok_pe
                p4["kwinw%d" % cur] = tok_te
                nkc = min(tg + 1, 3)
                Nk = nkc * 128
                woff = 384 - Nk
                for hp in range(8):
                    kb.wait("PE", p4.get("kwinw%d" % cur))
                    kb.wait("PE", tok_te)
                    kb.wait("PE", p4["exp"][0])
                    kb.wait("PE", p4["exp"][1])
                    kb.wait("PE", p4["norm"])
                    kb.op("PE", lambda p, tt=tt, hp=hp, Nk=Nk, woff=woff, cur=cur:
                          p.matmul(bnk(0, Nk), qTt[0:64, tt, hp, :],
                                   kwin[cur][0:64, hp, woff:384],
                                   start=True, stop=True, tile_position=(0, 0)))
                    mB = kb.op("PE", lambda p, tt=tt, hp=hp, Nk=Nk, woff=woff, cur=cur:
                               p.matmul(bnk(1, Nk), qTt[64:128, tt, hp, :],
                                        kwin[cur][64:128, hp, woff:384],
                                        start=True, stop=True, tile_position=(64, 0)),
                               inc="sPE")
                    kb.wait("DVE", mB)
                    kb.wait("DVE", p4["exp"][0])
                    kb.wait("DVE", p4["exp"][1])
                    srast = bass.AP(PA, 0, [[2048, 128], [512, 2], [1, Nk]])
                    t_n = kb.op("DVE", lambda v, srast=srast: v.reduce_max(
                        negm4[:, 0:2], srast, AX.X, negate=True), inc="sDVE")
                    kb.wait("ACT", t_n)
                    eA, eB2 = esb4[0], esb4[1]
                    kb.wait("ACT", p4["eb"][0])
                    kb.wait("ACT", p4["eb"][1])
                    p4["exp"][0] = kb.op("ACT", lambda a, eA=eA, Nk=Nk: a.activation(
                        eA[:, 0:Nk], bnk(0, Nk), AF.Exp, bias=negm4[:, 0:1]), inc="sACT")
                    p4["exp"][1] = kb.op("ACT", lambda a, eB2=eB2, Nk=Nk: a.activation(
                        eB2[:, 0:Nk], bnk(1, Nk), AF.Exp, bias=negm4[:, 1:2]), inc="sACT")
                    for hc in range(2):
                        et = eA if hc == 0 else eB2
                        obank = PA[0:65, 1024:1152] if hc == 0 else PA[0:65, 1536:1664]
                        ebf = ebuf4[hc]
                        kb.wait("PE", p4["exp"][hc])
                        kb.wait("PE", p4["eb"][0])
                        kb.wait("PE", p4["eb"][1])
                        for cc in range(nkc):
                            tok_t4 = kb.op("PE", lambda p, et=et, cc=cc: p.matmul(
                                PBb[:, cc * 1024:cc * 1024 + 128],
                                et[:, cc * 128:(cc + 1) * 128], idn[:, :],
                                is_transpose=True), inc="sPE")
                        kb.wait("DVE", tok_t4)
                        s4_ = pb_slots_b(nkc)
                        d4_ = bass.AP(ebf, 0, [[3 * 128, 128], [128, nkc], [1, 128]])
                        msk = mtri[:, 3 - nkc:3, :]
                        p4["eb"][hc] = kb.op("DVE", lambda v, s4_=s4_, d4_=d4_, msk=msk:
                                             v.tensor_tensor(d4_, s4_, msk, ALU.mult),
                                             inc="sDVE")
                        kb.wait("PE", p4["eb"][hc])
                        h = 2 * hp + hc
                        for cc in range(nkc):
                            p4["av"] = kb.op("PE", lambda p, kt_g=tg - (nkc - 1) + cc,
                                             h=h, ebf=ebf, cc=cc, obank=obank,
                                             first=(cc == 0), last=(cc == nkc - 1):
                                             p.matmul(obank, vh_i[:, kt_g % 6, h, 0:65],
                                                      ebf[:, cc, :],
                                                      start=first, stop=last), inc="sPE")
                    kb.wait("DVE", p4["av"])
                    kb.wait("DVE", p4["rb"])
                    rl_src = bass.AP(PA, 64 * 2048 + 1024, [[2048, 1], [512, 2], [1, 128]])
                    rl_dst = bass.AP(rlr4, 64 * 256, [[256, 1], [128, 2], [1, 128]])
                    t_rl = kb.op("DVE", lambda v: v.reciprocal(rl_dst, rl_src), inc="sDVE")
                    kb.wait("PE", t_rl)
                    t_rb = kb.op("PE", lambda p: p.matmul(
                        PB[0:64, 1536:1792], onesv[64:65, 0:64], rlr4[64:65, 0:256],
                        start=True, stop=True, tile_position=(64, 0)), inc="sPE")
                    p4["rb"] = t_rb
                    kb.wait("DVE", t_rb)
                    t_rbs = kb.op("DVE", lambda v: v.tensor_copy(
                        rbs4[:, :], PB[0:64, 1536:1792]), inc="sDVE")
                    kb.wait("DVE", t_rbs)
                    kb.wait("DVE", p4["oTi_used"])
                    o_src = bass.AP(PA, 1024, [[2048, 64], [512, 2], [1, 128]])
                    o_dst = bass.AP(oTi, (2 * hp) * 256 + (tt % 2) * 128,
                                    [[16 * 256, 64], [256, 2], [1, 128]])
                    rb_src = bass.AP(rbs4, 0, [[256, 64], [128, 2], [1, 128]])
                    p4["norm"] = kb.op("DVE", lambda v, o_dst=o_dst, o_src=o_src,
                                       rb_src=rb_src: v.tensor_tensor(
                        o_dst, o_src, rb_src, ALU.mult), inc="sDVE")
                if tt % 2 == 1:
                    ss = tt // 2
                    for oc in range(8):
                        wsl = woi_sl[oc % 2]
                        kb.wait("POOL", p4["woi"][oc % 2])
                        tok_ws = kb.dma("POOL", wsl[0:64, :, :],
                                        wco[L, :, oc * 128:(oc + 1) * 128].rearrange(
                                            "(j p) n -> p j n", p=64), "dWo4")
                        kb.wait("PE", tok_ws)
                        kb.wait("PE", p4["norm"])
                        kb.wait("PE", p4["zupd"])
                        for j in range(16):
                            kb.op("PE", lambda p, j=j, wsl=wsl, oc=oc: p.matmul(
                                bnk(2 + oc % 2, 256), wsl[0:64, j, :], oTi[0:64, j, :],
                                start=(j == 0), stop=(j == 15)))
                        tok_ym = kb.pe_tok()
                        kb.wait("DVE", tok_ym)
                        kb.wait("DVE", tok_bbs)
                        zcol = s * 512 + ss * 256 + 1
                        tok_zu = kb.op("DVE", lambda v, oc=oc, zcol=zcol: v.tensor_tensor(
                            z[:, oc, zcol:zcol + 256], z[:, oc, zcol:zcol + 256],
                            bnk(2 + oc % 2, 256), ALU.add), inc="sDVE")
                        p4["woi"][oc % 2] = tok_zu
                        p4["zupd"] = tok_zu
                    p4["oTi_used"] = p4["zupd"]
        last_cur = tgl(q, ZNS - 1, 3) % 2
        slot0 = tgl(q, ZNS - 1, 2) % 6
        kb.wait("SP", p4.get("kwinw%d" % last_cur))
        kb.wait("SP", p4["eb"][0])
        kb.wait("SP", p4["eb"][1])
        kb.dma("SP", ktl[L, :, :, :], kwin[last_cur][:, :, 128:384], "dOt4")
        kb.dma("SP", vtl[L, :, :, :, :], vh_i[:, slot0:slot0 + 2, :, :], "dOt4")

    def p4_ffn_stage(L, q):
        kb.wait("POOL", p4["b1"])
        kb.dma("POOL", lngb[0:1, :], lngd[L, :, :], "dWl4")
        t_l = kb.dma("POOL", lnbb[0:1, :], lnbd[L, :, :], "dWl4")
        t_b1 = kb.dma("SP", b1st[0:42, 0:128],
                      b1pd[L, 0:1, 0:5376].rearrange("o (c p) -> (o c) p", p=128), "dXb4")
        t_b1b = kb.dma("SP", b1st[42:44, 0:42],
                       b1pd[L, 0:1, 5376:5460].rearrange("o (c p) -> (o c) p", p=42),
                       "dXb4")
        t_b2 = kb.dma("SP", b2st[0:8, :],
                      b2d[L, 0:1, :].rearrange("o (c p) -> (o c) p", p=128), "dXb4")
        kb.wait("PE", t_b1)
        kb.wait("PE", t_b1b)
        kb.wait("PE", t_b2)
        kb.wait("PE", p4["stats"])
        kb.op("PE", lambda p: p.matmul(PB[0:128, 0:44], b1st[0:44, 0:128],
                                       idnf[0:44, 0:44], is_transpose=True))
        m2 = kb.op("PE", lambda p: p.matmul(PB[0:128, 512:520], b2st[0:8, :],
                                            idnf[0:8, 0:8], is_transpose=True), inc="sPE")
        kb.wait("DVE", m2)
        kb.op("DVE", lambda v: v.tensor_copy(b1col[:, :], PB[0:128, 0:44]))
        t_bc = kb.op("DVE", lambda v: v.tensor_copy(b2col[:, :], PB[0:128, 512:520]),
                     inc="sDVE")
        kb.wait("PE", t_bc)
        p4["b1"] = t_bc
        for s in range(ZNS):
            tok_zbc, tok_bbs = p4_zb_convert(L, 1, s)
            kb.wait("PE", tok_zbc)
            kb.wait("PE", t_bc)
            t_st = [None, None]
            for c in range(8):
                off = 1 if c < 4 else 0
                zt = zb2t[c % 2]
                kb.wait("ACT", t_st[c % 2])
                t_sqc = kb.op("ACT", lambda a, c=c, zt=zt: a.activation(
                    zt[:, :], zb[:, c, :], AF.Square), inc="sACT")
                kb.wait("PE", t_sqc)
                kb.op("PE", lambda p, c=c, off=off: p.matmul(
                    PB[0:1, 0:512], ones_col[:, :], zb[:, c, off:off + 512],
                    start=(c == 0), stop=(c == 7)))
                t_st[c % 2] = kb.op("PE", lambda p, c=c, off=off, zt=zt: p.matmul(
                    PB[0:1, 512:1024], ones_col[:, :], zt[:, off:off + 512],
                    start=(c == 0), stop=(c == 7)), inc="sPE")
            tok_st = t_st[1]
            kb.wait("DVE", tok_st)
            kb.op("DVE", lambda v: v.tensor_scalar(
                rowsc[0:1, 0:512], PB[0:1, 0:512], 1.0 / D, None, ALU.mult))
            kb.op("DVE", lambda v: v.tensor_scalar(
                rowsc[0:1, 512:1024], PB[0:1, 512:1024], 1.0 / D, None, ALU.mult))
            kb.op("DVE", lambda v: v.tensor_tensor(
                rowsc[0:1, 1024:1536], rowsc[0:1, 0:512], rowsc[0:1, 0:512], ALU.mult))
            kb.op("DVE", lambda v: v.tensor_tensor(
                rowsc[0:1, 1024:1536], rowsc[0:1, 512:1024], rowsc[0:1, 1024:1536],
                ALU.subtract))
            tvv = kb.op("DVE", lambda v: v.tensor_scalar(
                rowsc[0:1, 1024:1536], rowsc[0:1, 1024:1536], 0.0, None, ALU.max),
                inc="sDVE")
            kb.wait("ACT", tvv)
            kb.op("ACT", lambda a: a.activation(
                rowsc[0:1, 1024:1536], rowsc[0:1, 1024:1536], AF.Ln, bias=epsc[0:1, 0:1]))
            trr = kb.op("ACT", lambda a: a.activation(
                rowsc[0:1, 1024:1536], rowsc[0:1, 1024:1536], AF.Exp, scale=-0.5),
                inc="sACT")
            kb.wait("DVE", trr)
            kb.op("DVE", lambda v: v.tensor_copy(rowscb[0:1, 0:512], rowsc[0:1, 1024:1536]))
            kb.op("DVE", lambda v: v.tensor_tensor(
                rowsc[0:1, 1536:2048], rowsc[0:1, 0:512], rowsc[0:1, 1024:1536], ALU.mult))
            t_rows = kb.op("DVE", lambda v: v.tensor_scalar(
                rowscb[0:1, 512:1024], rowsc[0:1, 1536:2048], -1.0, None, ALU.mult),
                inc="sDVE")
            p4["stats"] = t_rows
            kb.wait("PE", t_rows)
            kb.wait("PE", t_l)
            for c in range(8):
                kb.wait("PE", p4["xnf"])
                kb.op("PE", lambda p, c=c: p.matmul(
                    PB[:, 1024:1536], lngb[0:1, c * 128:(c + 1) * 128],
                    rowscb[0:1, 0:512], start=True, stop=True))
                kb.op("PE", lambda p, c=c: p.matmul(
                    PB[:, 1536:2048], lngb[0:1, c * 128:(c + 1) * 128],
                    rowscb[0:1, 512:1024], start=True, stop=False))
                t_m = kb.op("PE", lambda p, c=c: p.matmul(
                    PB[:, 1536:2048], lnbb[0:1, c * 128:(c + 1) * 128],
                    ones_row[:, :], start=False, stop=True), inc="sPE")
                kb.wait("DVE", t_m)
                off = 1 if c < 4 else 0
                kb.op("DVE", lambda v, c=c, off=off: v.tensor_tensor(
                    zvt[:, :], zb[:, c, off:off + 512], PB[:, 1024:1536], ALU.mult))
                p4["xnf"] = kb.op("DVE", lambda v, c=c: v.tensor_tensor(
                    xn[:, c, :], zvt[:, :], PB[:, 1536:2048], ALU.add), inc="sDVE")
            p4["zb_used"] = p4["xnf"]
            kb.act_table(TBL_GEL)
            for pp in range(NP):
                pw = 128 if pp < 21 else 42
                sl = w1_sl[pp % 2]
                kb.wait("POOL", p4["w1"][pp % 2])
                tok_w1 = kb.dma("POOL", sl[:, :, 0:2 * pw],
                                wc1[L, :, pp * 256:pp * 256 + 2 * pw].rearrange(
                                    "(c p) n -> p c n", p=128), "dW14")
                kb.wait("PE", tok_w1)
                kb.wait("PE", p4["hTf"][pp])
                kb.wait("PE", p4["xnf"] if pp == 0 else None)
                bA4 = (pp % 2) * 2
                kb.wait("PE", p4["w1ga"][pp % 2])
                for kc in range(8):
                    kb.op("PE", lambda p, kc=kc, sl=sl, pw=pw, bA4=bA4:
                          p.matmul(PA[0:pw, bA4 * 512:bA4 * 512 + 512],
                                   sl[:, kc, 0:pw], xn[:, kc, :],
                                   start=(kc == 0), stop=(kc == 7)))
                for kc in range(8):
                    tok_a2 = kb.op("PE", lambda p, kc=kc, sl=sl, pw=pw, bA4=bA4:
                                   p.matmul(
                        PA[0:pw, bA4 * 512 + 512:bA4 * 512 + 1024],
                        sl[:, kc, pw:2 * pw], xn[:, kc, :],
                        start=(kc == 0), stop=(kc == 7)),
                        inc="sPE" if kc == 7 else None)
                kb.wait("ACT", tok_a2)
                gsl = ggs if pp % 2 == 0 else ggs2
                asl = aas if pp % 2 == 0 else aas2
                t_gg = kb.op("ACT", lambda a, pw=pw, pp=pp, bA4=bA4, gsl=gsl:
                             a.activation(
                    gsl[0:pw, :], PA[0:pw, bA4 * 512 + 512:bA4 * 512 + 1024], AF.Gelu,
                    bias=b1col[0:pw, 2 * pp + 1:2 * pp + 2]), inc="sACT")
                kb.wait("DVE", tok_a2)
                t_aa = kb.op("DVE", lambda v, pw=pw, pp=pp, bA4=bA4, asl=asl:
                             v.tensor_scalar(
                    asl[0:pw, :], PA[0:pw, bA4 * 512:bA4 * 512 + 512],
                    b1col[0:pw, 2 * pp:2 * pp + 1], None, ALU.add), inc="sDVE")
                kb.wait("POOL", t_gg)
                kb.wait("POOL", t_aa)
                tok_h = kb.op("POOL", lambda g, pw=pw, pp=pp, gsl=gsl, asl=asl:
                              g.tensor_tensor(
                    hT[0:pw, pp, :], asl[0:pw, :], gsl[0:pw, :], ALU.mult),
                    inc="sPOOL")
                p4["w1"][pp % 2] = tok_h
                p4["hTf"][pp] = tok_h
                p4["w1ga"][pp % 2] = tok_h
            kb.act_table(TBL_NLE)
            for oc in range(8):
                sl = w2_sl[oc % 2]
                kb.wait("POOL", p4["w2"][oc % 2])
                kb.dma("POOL", sl[:, 0:21, :],
                       wc2[L, 0:2688, oc * 128:(oc + 1) * 128].rearrange(
                           "(c p) n -> p c n", p=128), "dW24")
                tok_w2 = kb.dma("POOL", sl[0:42, 21, :],
                                wc2[L, 2688:2730, oc * 128:(oc + 1) * 128],
                                "dW24")
                kb.wait("PE", tok_w2)
                kb.wait("PE", p4["zupd"])
                kb.wait("PE", p4["w1ga"][0])
                kb.wait("PE", p4["w1ga"][1])
                for pp in range(NP):
                    pw = 128 if pp < 21 else 42
                    kb.wait("PE", p4["hTf"][pp] if oc == 0 else None)
                    kb.op("PE", lambda p, pp=pp, pw=pw, sl=sl, oc=oc: p.matmul(
                        bnk(2 + oc % 2), sl[0:pw, pp, :], hT[0:pw, pp, :],
                        start=(pp == 0), stop=(pp == NP - 1)))
                tok_y2 = kb.pe_tok()
                kb.wait("DVE", tok_y2)
                kb.wait("DVE", tok_bbs)
                zcol = s * 512 + 1
                tok_zu = kb.op("DVE", lambda v, oc=oc, zcol=zcol:
                               v.scalar_tensor_tensor(
                    z[:, oc, zcol:zcol + 512], bnk(2 + oc % 2),
                    b2col[:, oc:oc + 1], z[:, oc, zcol:zcol + 512],
                    ALU.add, ALU.add), inc="sDVE")
                p4["w2"][oc % 2] = tok_zu
                p4["zupd"] = tok_zu
            p4["w1"][0] = p4["zupd"]
            p4["w1"][1] = p4["zupd"]

    for q in range(NCH if "4" in phases else 0):
        kb.wait("SP", p4["zst"])
        tok_zl = kb.dma("SP", z[:, :, 1:ZCH + 1],
                        zSTs[q, :, :, :].rearrange("c p n -> p c n"), "dXz")
        kb.wait("DVE", tok_zl)
        tok_zr = kb.op("DVE", lambda v: v.memset(z[:, :, 0:1], 0.0), inc="sDVE")
        kb.wait("ACT", tok_zr)
        kb.wait("PE", tok_zr)
        for L in range(nl_run):
            for lt in p0["ltok"].get(L, []):
                kb.wait("POOL", lt)
            p4_tables(L, q)
            if q > 0:
                prv_last = (q * ZNT - 1) % 2
                kb.wait("POOL", p4["av"])
                kb.wait("POOL", p4["norm"])
                t_k1 = kb.dma("POOL", kwin[prv_last][:, :, 128:384], ktl[L, :, :, :],
                              "dXt4")
                sl0 = (q * ZNT - 2) % 6
                t_v1 = kb.dma("POOL", vh_i[:, sl0:sl0 + 2, :, :], vtl[L, :, :, :, :],
                              "dXt4")
                kb.wait("PE", t_v1)
                kb.wait("DVE", t_v1)
                kb.wait("PE", t_k1)
                kb.wait("DVE", t_k1)
                kb.wait("POOL", t_k1)
            p4_attn_stage(L, q)
            if not (attn_only_last and L == nl_run - 1):
                p4_ffn_stage(L, q)
        kb.wait("SP", p4["zupd"])
        tok_zs = kb.dma("SP", zSTs[q, :, :, :].rearrange("c p n -> p c n"),
                        z[:, :, 1:ZCH + 1], "dOz")
        p4["zst"] = tok_zs
        kb.barrier()

    es4.close()

    es5 = contextlib.ExitStack()
    z5 = sb(es5, "z5", [128, 8, ZCH], F32)
    zt5 = sb(es5, "zt5", [128, D], F32)
    x15 = [sb(es5, "x15a", [128, D], F32), sb(es5, "x15b", [128, D], F32)]
    ot5 = [sb(es5, "ot5a", [128, D], F32), sb(es5, "ot5b", [128, D], F32)]
    sq5 = sb(es5, "sq5", [128, D], F32)

    if "5" not in phases and not os.environ.get("K_DBG", ""):
        zro = sb(es5, "zro", [128, D], F32)
        tz = kb.op("DVE", lambda v: v.memset(zro[:, :], 0.0), inc="sDVE")
        kb.wait("SP", tz)
        for t in range(NT):
            kb.dma("SP", out[t * 128:(t + 1) * 128, :], zro[:, :], "dOut0")
    p5 = {"zl": None, "x": [None, None], "o": [None, None], "ev": None}
    for q in range(NCH if "5" in phases else 0):
        kb.wait("SP", p5["zl"])
        tok_z5 = kb.dma("SP", z5[:, :, :],
                        zSTs[q, :, :, :].rearrange("c p n -> p c n"), "dX5z")
        kb.wait("PE", tok_z5)
        for tq in range(ZNT):
            t = q * ZNT + tq
            kb.wait("SP", p5["x"][tq % 2])
            tok_x5 = kb.dma("SP", x15[tq % 2][:, :], x1s[t * 128:(t + 1) * 128, :],
                            "dX5x")
            for rnd in range(2):
                kb.wait("PE", p5["ev"])
                for c4 in range(4):
                    c = rnd * 4 + c4
                    tok_p5 = kb.op("PE", lambda p, c=c, c4=c4, tq=tq: p.matmul(
                        PB[:, c4 * 512:c4 * 512 + 128],
                        z5[:, c, tq * 128:(tq + 1) * 128], idnf[:, :],
                        is_transpose=True), inc="sPE")
                kb.wait("DVE", tok_p5)
                p5["ev"] = kb.op("DVE", lambda v, rnd=rnd: v.tensor_copy(
                    zt5[:, rnd * 512:(rnd + 1) * 512], pb_slots_f(4)), inc="sDVE")
            t_rs5 = kb.op("DVE", lambda v: v.reduce_sum(tinyA[:, 0:1], zt5[:, :], AX.X),
                          inc="sDVE")
            kb.wait("ACT", t_rs5)
            tsq5 = kb.op("ACT", lambda a: a.activation(
                sq5[:, :], zt5[:, :], AF.Square, accum_out=tinyA[:, 1:2]), inc="sACT")
            kb.wait("DVE", tsq5)
            p3_rstd(0, 11)
            kb.wait("DVE", tok_x5)
            kb.wait("DVE", p5["o"][tq % 2])
            kb.op("DVE", lambda v: v.tensor_scalar(
                sq5[:, :], zt5[:, :], tinyA[:, 10:11], tinyA[:, 11:12],
                ALU.subtract, ALU.mult))
            tok_o5 = kb.op("DVE", lambda v, tq=tq: v.tensor_tensor(
                ot5[tq % 2][:, :], sq5[:, :], x15[tq % 2][:, :], ALU.add), inc="sDVE")
            p5["x"][tq % 2] = tok_o5
            kb.wait("SP", tok_o5)
            p5["o"][tq % 2] = kb.dma("SP", out[t * 128:(t + 1) * 128, :],
                                     ot5[tq % 2][:, :], "dOut%d" % (tq % 2))
        p5["zl"] = p5["ev"]

    DBG = os.environ.get("K_DBG", "")
    if DBG:
        for s, v in list(kb.cnt.items()):
            if s.startswith("d"):
                kb.wait("SP", (s, v))
        if DBG == "x1":
            kb.dma("SP", out[:, :], x1s[:, :], "dDbg")
        elif DBG == "z":
            kb.dma("SP", out[:, :].rearrange("(q z) (c p) -> q c p z", q=NCH, c=8),
                   zSTs[:, :, :, :], "dDbg")
        elif DBG == "o":
            kb.dma("POOL", out[:, :].rearrange("(h d) t -> h d t", h=16),
                   oT1s[:, :, :], "dDbg")

    kb.finalize_and_emit()
    return kb


def _host_consts(T):
    NT = T // 128
    idn = np.eye(128, dtype=ml_dtypes.bfloat16)
    idnf = np.eye(128, dtype=np.float32)
    utri = np.triu(np.ones((128, 128), np.float32))
    ones = np.ones((128, 128), np.float32)
    ltri = np.tril(np.ones((128, 128), np.float32))
    mtri = np.stack([ltri, ones, utri], axis=1).astype(ml_dtypes.bfloat16)
    half = np.arange(0, DH, 2, dtype=np.float64)
    inv_freq = 1.0 / 10000.0 ** (half / DH)
    t = np.arange(T, dtype=np.float64)
    freqs = t[:, None] * inv_freq[None, :]
    base_scale = (half + 0.4 * DH) / (1.4 * DH)
    power = (t - T // 2) / (256 // 2)
    scale = base_scale[None, :] ** power[:, None]
    cos = np.cos(freqs)
    sin = np.sin(freqs)
    cq = np.concatenate([cos * scale, cos * scale], 1)
    sq = np.concatenate([-sin * scale, sin * scale], 1)
    ck = np.concatenate([cos / scale, cos / scale], 1)
    sk = np.concatenate([-sin / scale, sin / scale], 1)
    rot = np.stack([cq, sq, ck, sk], 0).astype(np.float32)
    rot = np.ascontiguousarray(rot.reshape(4, NT, 128, DH).transpose(0, 2, 1, 3))
    return idn, idnf, mtri, rot


def _w1_perm():
    perm = []
    for p in range(NP):
        a0, a1 = p * 128, min((p + 1) * 128, DI)
        perm.extend(range(a0, a1))
        perm.extend(range(DI + a0, DI + a1))
    return np.array(perm, dtype=np.int64)


def kernel(L0_attn_x, L0_attn_Wq, L0_attn_bq, L0_attn_Wk, L0_attn_bk, L0_attn_Wv,
           L0_attn_bv, L0_attn_Wo, L0_attn_bo, L0_attn_gamma, L0_attn_beta,
           L0_ffn_Wqkv, L0_ffn_q_scale, L0_ffn_k_scale, L0_ffn_Wo, L0_ffn_ln_g,
           L0_ffn_ln_b, L0_ffn_W1, L0_ffn_b1, L0_ffn_W2, L0_ffn_b2):
    x = np.ascontiguousarray(np.asarray(L0_attn_x, dtype=np.float32))
    B, T, _ = x.shape
    idn, idnf, mtri, rot = _host_consts(T)
    perm = _w1_perm()
    f32 = lambda a: np.ascontiguousarray(np.asarray(a, dtype=np.float32))
    W1perm = np.ascontiguousarray(np.asarray(L0_ffn_W1, np.float32)[:, :, perm])
    b1perm = np.ascontiguousarray(np.asarray(L0_ffn_b1, np.float32)[:, perm][:, None, :])
    common = {
        "Wq": f32(L0_attn_Wq), "Wk": f32(L0_attn_Wk), "Wv": f32(L0_attn_Wv),
        "Wo": f32(L0_attn_Wo),
        "bqkvo": np.ascontiguousarray(np.stack(
            [f32(L0_attn_bq), f32(L0_attn_bk), f32(L0_attn_bv), f32(L0_attn_bo)],
            0)[:, None, :]),
        "gamd": f32(L0_attn_gamma)[None, :], "betd": f32(L0_attn_beta)[None, :],
        "Wqkv": f32(L0_ffn_Wqkv), "qscd": f32(L0_ffn_q_scale)[:, None, :],
        "kscd": f32(L0_ffn_k_scale)[:, None, :], "Woi": f32(L0_ffn_Wo),
        "lngd": f32(L0_ffn_ln_g)[:, None, :], "lnbd": f32(L0_ffn_ln_b)[:, None, :],
        "W1p": W1perm, "b1pd": b1perm, "W2d": f32(L0_ffn_W2),
        "b2d": f32(L0_ffn_b2)[:, None, :],
        "idnd": idn, "idnfd": idnf, "mtrid": mtri, "rotd": rot,
    }
    kbk = build_nc(T)
    NCH = T // 512
    scratch = {
        "x1s": np.zeros((T, D), np.float32),
        "zSTs": np.zeros((NCH, 8, 128, 512), np.float32),
        "oT1s": np.zeros((16, 64, T), ml_dtypes.bfloat16),
        "ktl": np.zeros((NL, 128, 8, 256), ml_dtypes.bfloat16),
        "vtl": np.zeros((NL, 128, 2, H, 66), ml_dtypes.bfloat16),
        "wcq": np.zeros((NL, D, 3 * D), ml_dtypes.bfloat16),
        "wc1": np.zeros((NL, D, 2 * DI), ml_dtypes.bfloat16),
        "wc2": np.zeros((NL, DI, D), ml_dtypes.bfloat16),
        "wco": np.zeros((NL, D, D), ml_dtypes.bfloat16),
    }
    common.update(scratch)
    in_maps = []
    for b in range(B):
        m = dict(common)
        m["x0"] = np.ascontiguousarray(x[b])
        in_maps.append(m)
    res = run_bass_kernel_spmd(kbk.nc, in_maps, core_ids=list(range(B)))
    return np.stack([np.asarray(res.results[b]["out"]) for b in range(B)], 0)
